# version 15 (final)
# baseline (speedup 1.0000x reference)
import numpy as np
from contextlib import ExitStack

import concourse.bass as bass
import concourse.mybir as mybir
from concourse.bass_utils import run_bass_kernel_spmd

F32 = mybir.dt.float32
BF16 = mybir.dt.bfloat16
AF = mybir.ActivationFunctionType
ALU = mybir.AluOpType

D = 4096
KT = D // 128
FD = 11008
FT = FD // 128
R = 1024
NCORES = 8
CHUNKS = (15, 15, 14, 14, 14, 14)
EPS = 1e-5

ENGINES = ("tensor", "scalar", "vector", "gpsimd", "sync")

DMA_SEMS = ("wgu0", "wgu1", "wd0", "wd1", "wd2", "xc0", "xc1", "xc2", "xc3",
            "xin0", "xin1", "yin0", "yin1", "yin2", "st0", "st1")
CMP_SEMS = ("pe", "act", "dve", "pl")


class Prog:

    def __init__(self):
        self.ops = {e: [] for e in ENGINES}
        self.cnt = {s: 0 for s in DMA_SEMS + CMP_SEMS}

    def emit(self, eng, fn=None, waits=(), inc=None, n=1):
        mile = None
        if inc is not None:
            if inc in DMA_SEMS and self.cnt[inc] > 0:
                waits = tuple(waits) + ((inc, self.cnt[inc]),)
            self.cnt[inc] += n
            mile = (inc, self.cnt[inc])
        w = {}
        for s, v in waits:
            if v is not None and (s not in w or v > w[s]):
                w[s] = v
        self.ops[eng].append((fn, tuple(w.items()), (inc, n) if inc else None))
        return mile


class Free:

    def __init__(self):
        self.d = {}

    def need(self, key):
        return tuple(self.d.get(key, {}).items())

    def reader(self, key, mile):
        if mile is None:
            return
        s, v = mile
        m = self.d.setdefault(key, {})
        m[s] = max(m.get(s, 0), v)

    def wrote(self, key):
        self.d[key] = {}


def build_nc(r=R, ft=FT, kt=KT, chunks=CHUNKS):
    assert sum(chunks) == ft
    rh_n = r // 512
    maxL = max(chunks)
    d = kt * 128

    nc = bass.Bass("TRN2", target_bir_lowering=False)

    xT = nc.dram_tensor("xT", [d, r], F32, kind="ExternalInput").ap()
    wgu = nc.dram_tensor("wgu", [ft, 128, 2, kt, 128], F32, kind="ExternalInput").ap()
    wd = nc.dram_tensor("wd", [kt, 128, ft, 128], F32, kind="ExternalInput").ap()
    outT = nc.dram_tensor("outT", [d, r], F32, kind="ExternalOutput").ap()
    x1T = nc.dram_tensor("x1T", [d, r], F32, kind="Internal").ap()
    yP = nc.dram_tensor("yP", [d, r], F32, kind="Internal").ap()

    P = Prog()
    fr = Free()
    store_mile = {}

    with ExitStack() as ctx:
        sb = lambda *a, **k: ctx.enter_context(nc.sbuf_tensor(*a, **k))
        ps = lambda *a: ctx.enter_context(nc.psum_tensor(*a))
        hT = sb("hT", [128, kt * r], BF16)
        gu = sb("gu", [128, maxL * r], BF16)
        wgus = [sb(f"wgus{i}", [128, 2 * kt * 128], BF16) for i in range(2)]
        wds = [sb(f"wds{i}", [128, maxL * 128], BF16) for i in range(3)]
        xin = [sb(f"xin{i}", [128, r], F32) for i in range(2)]
        sq = [sb(f"sq{i}", [128, r], BF16) for i in range(2)]
        silu = sb("silu", [128, r], F32)
        yin = [sb(f"yin{i}", [128, r], F32) for i in range(3)]
        st = [sb(f"stg{i}", [128, r], F32) for i in range(2)]
        mb = sb("mb", [128, r], F32)
        rsb = sb("rsb", [128, r], F32)
        mb2 = sb("mb2", [128, r], F32)
        rsb2 = sb("rsb2", [128, r], F32)
        srows = sb("srows", [128, 4 * r], F32)
        onesb = sb("onesb", [128, 1], BF16)
        onesf = sb("onesf", [1, 128], F32)
        Pb = [ps(f"P{i}", [128, r], F32) for i in range(4)]

        sems = {}
        for s in DMA_SEMS + CMP_SEMS:
            sems[s] = ctx.enter_context(nc.semaphore(s))

        R0 = srows[0:1, 0:r]
        R1 = srows[0:1, r:2 * r]
        R2 = srows[0:1, 2 * r:3 * r]
        R3 = srows[0:1, 3 * r:4 * r]

        P.emit("vector", lambda v: v.memset(onesb[:], 1.0))
        m_init = P.emit("vector", lambda v: v.memset(onesf[:], 1.0), inc="dve")

        def make_stats(sumP, sumK, sqP, sqK):
            state = {"last": None}

            def feed(dt, a, akey, amile):
                slot = dt % 2
                msq = P.emit("scalar",
                             (lambda a=a, slot=slot: lambda s:
                              s.activation(sq[slot][:], a, AF.Square))(),
                             waits=(amile,) + fr.need(f"sq{slot}"), inc="act")
                fr.wrote(f"sq{slot}")
                fr.reader(akey, msq)
                mm = None
                for h in range(rh_n):
                    c0, c1 = h * 512, (h + 1) * 512
                    mm = P.emit("tensor",
                                (lambda a=a, c0=c0, c1=c1, dt=dt: lambda t:
                                 t.matmul(sumP[0:1, c0:c1], onesb[:], a[:, c0:c1],
                                          start=(dt == 0), stop=(dt == kt - 1),
                                          skip_group_check=True))(),
                                waits=(amile, m_init) + (fr.need(sumK) if dt == 0 and h == 0 else ()),
                                inc="pe" if h == rh_n - 1 else None)
                fr.reader(akey, mm)
                mm2 = None
                for h in range(rh_n):
                    c0, c1 = h * 512, (h + 1) * 512
                    mm2 = P.emit("tensor",
                                 (lambda slot=slot, c0=c0, c1=c1, dt=dt: lambda t:
                                  t.matmul(sqP[0:1, c0:c1], onesb[:], sq[slot][:, c0:c1],
                                           start=(dt == 0), stop=(dt == kt - 1),
                                           skip_group_check=True))(),
                                 waits=(msq,) + (fr.need(sqK) if dt == 0 and h == 0 else ()),
                                 inc="pe" if h == rh_n - 1 else None)
                fr.reader(f"sq{slot}", mm2)
                state["last"] = mm2
                if dt == 0:
                    fr.wrote(sumK)
                    fr.wrote(sqK)

            return feed, state

        def _row_ts(dst, src, s1, s2, op1, op2, waits, dk, sks=()):
            if op2 is None:
                fn = (lambda dst=dst, src=src: lambda v: v.tensor_scalar(
                    dst, src, s1, None, op1))()
            else:
                fn = (lambda dst=dst, src=src: lambda v: v.tensor_scalar(
                    dst, src, s1, s2, op1, op2))()
            m = P.emit("vector", fn, waits=waits + fr.need(dk), inc="dve")
            fr.wrote(dk)
            for sk in sks:
                fr.reader(sk, m)
            return m

        def _row_tt(dst, a, b, op, waits, dk, sks=()):
            m = P.emit("vector",
                       (lambda dst=dst, a=a, b=b: lambda v: v.tensor_tensor(
                           dst, a, b, op))(),
                       waits=waits + fr.need(dk), inc="dve")
            fr.wrote(dk)
            for sk in sks:
                fr.reader(sk, m)
            return m

        def _bcast(row, target, tkey, wait, via):
            viaP, viaK = via
            bm = None
            for h in range(rh_n):
                c0, c1 = h * 512, (h + 1) * 512
                bm = P.emit("tensor",
                            (lambda c0=c0, c1=c1, row=row, viaP=viaP: lambda t:
                             t.matmul(viaP[:, c0:c1], onesf[:], row[:, c0:c1],
                                      start=True, stop=True,
                                      skip_group_check=True))(),
                            waits=(wait, m_init) + (fr.need(viaK) if h == 0 else ()),
                            inc="pe" if h == rh_n - 1 else None)
            fr.wrote(viaK)
            cm = P.emit("vector",
                        (lambda target=target, viaP=viaP: lambda v:
                         v.tensor_copy(target[:], viaP[:]))(),
                        waits=(bm,) + fr.need(tkey), inc="dve")
            fr.wrote(tkey)
            fr.reader(viaK, cm)
            return cm

        def rows_h(last_mm, sumP, sumK, sqP, sqK):
            inv_n = 1.0 / float(d)
            mM = P.emit("vector", lambda v: v.tensor_scalar(
                R0, sumP[0:1, :], inv_n, None, ALU.mult),
                waits=(last_mm,) + fr.need("R0"), inc="dve")
            fr.wrote("R0")
            fr.reader(sumK, mM)
            mQ = P.emit("vector", lambda v: v.tensor_scalar(
                R1, sqP[0:1, :], inv_n, None, ALU.mult),
                waits=(last_mm,) + fr.need("R1"), inc="dve")
            fr.wrote("R1")
            fr.reader(sqK, mQ)
            m2 = _row_tt(R2, R0, R0, ALU.mult, (mM,), "R2")
            mV = _row_tt(R1, R1, R2, ALU.subtract, (mQ, m2), "R1", ("R2",))
            mVe = _row_ts(R2, R1, 1.0, EPS, ALU.mult, ALU.add, (mV,), "R2", ("R1",))
            mS = P.emit("scalar", lambda s: s.activation(R3, R2, AF.Sqrt),
                        waits=(mVe,) + fr.need("R3"), inc="act")
            fr.wrote("R3")
            fr.reader("R2", mS)
            mR = P.emit("vector", lambda v: v.reciprocal(R2, R3),
                        waits=(mS,) + fr.need("R2"), inc="dve")
            fr.wrote("R2")
            fr.reader("R3", mR)
            mE = _row_tt(R3, R0, R2, ALU.mult, (mR,), "R3", ("R0", "R2"))
            ca = _bcast(R2, rsb, "rsb", mR, (Pb[2], "P2"))
            fr.reader("R2", ca)
            cb = _bcast(R3, mb, "mb", mE, (Pb[3], "P3"))
            fr.reader("R3", cb)
            return ca, cb

        def rows_mixer(last_mm, sumP, sumK, sqP, sqK):
            inv_n = 1.0 / float(d)
            mM = P.emit("vector", lambda v: v.tensor_scalar(
                R0, sumP[0:1, :], inv_n, None, ALU.mult),
                waits=(last_mm,) + fr.need("R0"), inc="dve")
            fr.wrote("R0")
            fr.reader(sumK, mM)
            mQ = P.emit("vector", lambda v: v.tensor_scalar(
                R1, sqP[0:1, :], inv_n, None, ALU.mult),
                waits=(last_mm,) + fr.need("R1"), inc="dve")
            fr.wrote("R1")
            fr.reader(sqK, mQ)
            m2_ = _row_tt(R2, R0, R0, ALU.mult, (mM,), "R2")
            mV = _row_tt(R1, R1, R2, ALU.subtract, (mQ, m2_), "R1", ("R2",))
            mVe = _row_ts(R2, R1, 1.0, EPS, ALU.mult, ALU.add, (mV,), "R2", ("R1",))
            mS = P.emit("scalar", lambda s: s.activation(R3, R2, AF.Sqrt),
                        waits=(mVe,) + fr.need("R3"), inc="act")
            fr.wrote("R3")
            fr.reader("R2", mS)
            mR = P.emit("vector", lambda v: v.reciprocal(R2, R3),
                        waits=(mS,) + fr.need("R2"), inc="dve")
            fr.wrote("R2")
            fr.reader("R3", mR)
            m2rs = _row_ts(R3, R2, 2.0, None, ALU.mult, None, (mR,), "R3", ("R2",))
            mB = _row_tt(R2, R0, R3, ALU.mult, (m2rs,), "R2")
            mA = _row_ts(R3, R3, 1.0, None, ALU.add, None, (mB,), "R3")
            cA = _bcast(R3, rsb2, "rsb2", mA, (Pb[2], "P2"))
            cB = _bcast(R2, mb2, "mb2", mB, (Pb[3], "P3"))
            mm1 = _row_tt(R0, R0, R3, ALU.mult, (mA,), "R0")
            mm2 = _row_tt(R0, R0, R2, ALU.subtract, (mm1,), "R0")
            mv1 = _row_tt(R1, R1, R3, ALU.mult, (mV, mA), "R1")
            mv2 = _row_tt(R1, R1, R3, ALU.mult, (mv1,), "R1")
            mbm = _row_tt(R2, R2, R0, ALU.add, (mm2, cB), "R2", ("R0",))
            mve2 = _row_ts(R0, R1, 1.0, EPS, ALU.mult, ALU.add, (mv2,), "R0", ("R1",))
            mS2 = P.emit("scalar", lambda s: s.activation(R1, R0, AF.Sqrt),
                         waits=(mve2,) + fr.need("R1"), inc="act")
            fr.wrote("R1")
            fr.reader("R0", mS2)
            mR2 = P.emit("vector", lambda v: v.reciprocal(R0, R1),
                         waits=(mS2,) + fr.need("R0"), inc="dve")
            fr.wrote("R0")
            fr.reader("R1", mR2)
            mC = _row_tt(R3, R3, R0, ALU.mult, (mR2, cA), "R3", ("R0",))
            mE = _row_tt(R2, R2, R0, ALU.mult, (mbm, mR2), "R2", ("R0",))
            ca = _bcast(R3, rsb, "rsb", mC, (Pb[2], "P2"))
            fr.reader("R3", ca)
            cb = _bcast(R2, mb, "mb", mE, (Pb[3], "P3"))
            fr.reader("R2", cb)
            return ca, cb

        def h_inplace(bc_miles):
            ca, cb = bc_miles
            miles = []
            for dt in range(kt):
                hsl = hT[:, dt * r:(dt + 1) * r]
                m1 = P.emit("vector",
                            (lambda hsl=hsl: lambda v: v.tensor_tensor(
                                silu[:], hsl, rsb[:], ALU.mult))(),
                            waits=(ca,) + fr.need("silu") + fr.need(("hTw", dt)),
                            inc="dve")
                fr.wrote("silu")
                fr.reader("rsb", m1)
                m2 = P.emit("vector",
                            (lambda hsl=hsl: lambda v: v.tensor_tensor(
                                hsl, silu[:], mb[:], ALU.subtract))(),
                            waits=(m1, cb), inc="dve")
                fr.reader("silu", m2)
                fr.reader("mb", m2)
                miles.append(m2)
            return miles

        pre_wgu = {}

        def emit_wgu(phase, ftile):
            if (phase, ftile) in pre_wgu:
                return pre_wgu.pop((phase, ftile))
            slot = ftile % 2
            m = P.emit(
                "gpsimd",
                (lambda ftile=ftile, slot=slot: lambda g:
                 g.dma_start(wgus[slot][:], wgu[ftile].rearrange(
                     "dd m k f -> dd (m k f)")))(),
                waits=fr.need(f"wgus{slot}"), inc=f"wgu{slot}", n=16)
            fr.wrote(f"wgus{slot}")
            return m

        def prefetch_wgu(phase, ftile):
            pre_wgu[(phase, ftile)] = emit_wgu(phase, ftile)

        def main_loop(phase, res_src, res_name, dst, dst_name, hT_miles,
                      res_affine=False, on_last_chunk_dt=None,
                      on_last_chunk_dt_pre=None, after_loop_pe=None):
            hT_last = hT_miles[-1]
            wd_mile = {}

            def wds_dma(c, base, L, dt):
                slot = dt % 3
                wd_mile[dt] = P.emit(
                    "gpsimd",
                    (lambda dt=dt, base=base, L=L, slot=slot: lambda g:
                     g.dma_start(wds[slot][:, 0:L * 128],
                                 wd[dt, :, base:base + L, :].rearrange(
                                     "ff l dd -> ff (l dd)")))(),
                    waits=fr.need(f"wds{slot}"), inc=f"wd{slot}", n=16)
                fr.wrote(f"wds{slot}")

            for c, L in enumerate(chunks):
                base = sum(chunks[:c])
                first_c, last_c = (c == 0), (c == len(chunks) - 1)
                fused = {}
                for j in range(L):
                    ftile = base + j
                    slot = ftile % 2
                    pg = Pb[0] if ftile % 2 == 0 else Pb[2]
                    pu = Pb[1] if ftile % 2 == 0 else Pb[3]
                    pgk = "P0" if ftile % 2 == 0 else "P2"
                    puk = "P1" if ftile % 2 == 0 else "P3"
                    if first_c and j == 0:
                        mw = [emit_wgu(phase, base), emit_wgu(phase, base + 1)]
                        tanks = [(0, "g", Pb[0], "P0", 0, 0), (0, "u", Pb[1], "P1", kt, 0),
                                 (1, "g", Pb[2], "P2", 0, 1), (1, "u", Pb[3], "P3", kt, 1)]
                        for k in range(kt):
                            for jj, which, pp, ppk, koff, wslot in tanks:
                                for h in range(rh_n):
                                    c0, c1 = h * 512, (h + 1) * 512
                                    if k == 0 and h == 0:
                                        w = (mw[jj], hT_miles[0]) + fr.need(ppk)
                                    elif h == 0 and jj == 0 and which == "g":
                                        w = (hT_miles[k],)
                                    else:
                                        w = ()
                                    fused[(jj, which)] = P.emit(
                                        "tensor",
                                        (lambda wslot=wslot, k=k, c0=c0, c1=c1,
                                                pp=pp, koff=koff: lambda t: t.matmul(
                                            pp[:, c0:c1],
                                            wgus[wslot][:, (koff + k) * 128:(koff + k + 1) * 128],
                                            hT[:, k * r + c0:k * r + c1],
                                            start=(k == 0), stop=(k == kt - 1),
                                            skip_group_check=True))(),
                                        waits=w,
                                        inc="pe" if (k == kt - 1 and h == rh_n - 1) else None)
                        for _, _, _, ppk, _, _ in tanks:
                            fr.wrote(ppk)
                    if first_c and j in (0, 1):
                        mg = fused[(j, "g")]
                        mu = fused[(j, "u")]
                    else:
                        mwgu = emit_wgu(phase, ftile)
                        for which, pp, ppk, koff in (("g", pg, pgk, 0),
                                                     ("u", pu, puk, kt)):
                            mlast = None
                            for k in range(kt):
                                for h in range(rh_n):
                                    c0, c1 = h * 512, (h + 1) * 512
                                    if k == 0 and h == 0:
                                        w = (mwgu, hT_last) + fr.need(ppk)
                                    else:
                                        w = ()
                                    mlast = P.emit(
                                        "tensor",
                                        (lambda slot=slot, k=k, c0=c0, c1=c1,
                                                pp=pp, koff=koff: lambda t: t.matmul(
                                            pp[:, c0:c1],
                                            wgus[slot][:, (koff + k) * 128:(koff + k + 1) * 128],
                                            hT[:, k * r + c0:k * r + c1],
                                            start=(k == 0), stop=(k == kt - 1),
                                            skip_group_check=True))(),
                                        waits=w,
                                        inc="pe" if (k == kt - 1 and h == rh_n - 1) else None)
                            fr.wrote(ppk)
                            if which == "g":
                                mg = mlast
                            else:
                                mu = mlast
                    fr.reader(f"wgus{slot}", mu)
                    fr.reader("hT", mu)
                    ms = P.emit("scalar",
                                (lambda pg=pg: lambda s: s.activation(
                                    silu[:], pg[:], AF.Silu))(),
                                waits=(mg,) + fr.need("silu"), inc="act")
                    fr.wrote("silu")
                    fr.reader(pgk, ms)
                    mgu = P.emit("vector",
                                 (lambda j=j, pu=pu: lambda v: v.tensor_tensor(
                                     gu[:, j * r:(j + 1) * r], silu[:], pu[:],
                                     ALU.mult))(),
                                 waits=(ms, mu) + (fr.need("gu") if j == 0 else ()),
                                 inc="dve")
                    if j == 0:
                        fr.wrote("gu")
                        gu_miles = []
                    gu_miles.append(mgu)
                    fr.reader("silu", mgu)
                    fr.reader(puk, mgu)
                    if j == L - 1:
                        gu_done = mgu
                        wds_dma(c, base, L, 0)
                        wds_dma(c, base, L, 1)
                        wds_dma(c, base, L, 2)
                        if not last_c:
                            prefetch_wgu(phase, base + L)
                            prefetch_wgu(phase, base + L + 1)
                        elif phase == 0:
                            prefetch_wgu(1, 0)
                            prefetch_wgu(1, 1)
                yin_mile = {}

                def yin_dma(dt):
                    yslot = dt % 3
                    yin_mile[dt] = P.emit(
                        "sync",
                        (lambda dt=dt, yslot=yslot: lambda s: s.dma_start(
                            yin[yslot][:], yP[dt * 128:(dt + 1) * 128, :]))(),
                        waits=fr.need(f"yin{yslot}") + (store_mile[("yP", dt)],),
                        inc=f"yin{yslot}", n=16)
                    fr.wrote(f"yin{yslot}")
                    fr.reader(("yP", dt), yin_mile[dt])

                for dt in range(kt):
                    slot = dt % 2
                    yslot = dt % 3
                    mwd = wd_mile[dt]
                    if not first_c:
                        if dt == 0:
                            yin_dma(0)
                            yin_dma(1)
                        if dt + 2 < kt:
                            yin_dma(dt + 2)
                        myin = yin_mile[dt]
                    if last_c:
                        dep = store_mile.get((res_name, dt))
                        mxin = P.emit(
                            "scalar",
                            (lambda dt=dt, slot=slot: lambda s: s.dma_start(
                                xin[slot][:],
                                res_src[dt * 128:(dt + 1) * 128, :]))(),
                            waits=fr.need(f"xin{slot}") + ((dep,) if dep else ()),
                            inc=f"xin{slot}", n=16)
                        fr.wrote(f"xin{slot}")
                        if res_affine:
                            q1 = P.emit("vector",
                                        (lambda slot=slot: lambda v:
                                         v.tensor_tensor(xin[slot][:], xin[slot][:],
                                                         rsb2[:], ALU.mult))(),
                                        waits=(mxin,), inc="dve")
                            fr.reader("rsb2", q1)
                            mxin = P.emit("vector",
                                          (lambda slot=slot: lambda v:
                                           v.tensor_tensor(xin[slot][:], xin[slot][:],
                                                           mb2[:], ALU.subtract))(),
                                          waits=(q1,), inc="dve")
                            fr.reader("mb2", mxin)
                    py = Pb[0] if dt % 2 == 0 else Pb[2]
                    pyk = "P0" if dt % 2 == 0 else "P2"
                    wslot = dt % 3
                    my = None
                    wz = (mwd, gu_miles[0] if dt == 0 else gu_done) + fr.need(pyk)
                    for j in range(L):
                        for h in range(rh_n):
                            c0, c1 = h * 512, (h + 1) * 512
                            if j == 0 and h == 0:
                                w = wz
                            elif dt == 0 and h == 0:
                                w = (gu_miles[j],)
                            else:
                                w = ()
                            my = P.emit(
                                "tensor",
                                (lambda wslot=wslot, j=j, c0=c0, c1=c1, py=py:
                                 lambda t: t.matmul(
                                     py[:, c0:c1],
                                     wds[wslot][:, j * 128:(j + 1) * 128],
                                     gu[:, j * r + c0:j * r + c1],
                                     start=(j == 0), stop=(j == L - 1),
                                     skip_group_check=True))(),
                                waits=w,
                                inc="pe" if (j == L - 1 and h == rh_n - 1) else None)
                    fr.wrote(pyk)
                    fr.reader(f"wds{wslot}", my)
                    fr.reader("gu", my)
                    if dt + 3 < kt:
                        wds_dma(c, base, L, dt + 3)
                    if last_c and on_last_chunk_dt_pre is not None:
                        on_last_chunk_dt_pre(dt)
                    sslot = dt % 2
                    if first_c and not last_c:
                        mev = P.emit("scalar",
                                     (lambda py=py, sslot=sslot: lambda s:
                                      s.activation(st[sslot][:], py[:], AF.Copy,
                                                   scale=0.5))(),
                                     waits=(my,) + fr.need(f"stg{sslot}"),
                                     inc="act")
                        fr.wrote(f"stg{sslot}")
                        fr.reader(pyk, mev)
                    else:
                        hbuf, hkey = (silu, "silu") if dt % 2 == 0 else (mb, "mb")
                        mhalf = P.emit("scalar",
                                       (lambda py=py, hbuf=hbuf: lambda s:
                                        s.activation(hbuf[:], py[:], AF.Copy,
                                                     scale=0.5))(),
                                       waits=(my,) + fr.need(hkey), inc="act")
                        fr.wrote(hkey)
                        fr.reader(pyk, mhalf)
                        if not last_c:
                            mev = P.emit("vector",
                                         (lambda sslot=sslot, yslot=yslot, hbuf=hbuf:
                                          lambda v: v.tensor_tensor(
                                              st[sslot][:], hbuf[:], yin[yslot][:],
                                              ALU.add))(),
                                         waits=(mhalf, myin) + fr.need(f"stg{sslot}"),
                                         inc="dve")
                            fr.wrote(f"stg{sslot}")
                            fr.reader(f"yin{yslot}", mev)
                            fr.reader(hkey, mev)
                        else:
                            if not first_c:
                                ma = P.emit("vector",
                                            (lambda yslot=yslot, hbuf=hbuf: lambda v:
                                             v.tensor_tensor(hbuf[:], hbuf[:],
                                                             yin[yslot][:], ALU.add))(),
                                            waits=(mhalf, myin), inc="dve")
                                fr.reader(f"yin{yslot}", ma)
                            else:
                                ma = mhalf
                            mev = P.emit("vector",
                                         (lambda sslot=sslot, slot=slot, hbuf=hbuf:
                                          lambda v: v.tensor_tensor(
                                              st[sslot][:], hbuf[:],
                                              xin[slot][:], ALU.add))(),
                                         waits=(ma, mxin) + fr.need(f"stg{sslot}"),
                                         inc="dve")
                            fr.reader(f"xin{slot}", mev)
                            fr.reader(hkey, mev)
                            fr.wrote(f"stg{sslot}")
                    tgt, tname = (yP, "yP") if not last_c else (dst, dst_name)
                    wst = fr.need((tname, dt)) if tname == "yP" else ()
                    mst = P.emit(
                        "sync",
                        (lambda tgt=tgt, dt=dt, sslot=sslot: lambda s:
                         s.dma_start(tgt[dt * 128:(dt + 1) * 128, :],
                                     st[sslot][:]))(),
                        waits=(mev,) + wst, inc=f"st{sslot}", n=16)
                    store_mile[(tname, dt)] = mst
                    if tname == "yP":
                        fr.wrote((tname, dt))
                    fr.reader(f"stg{sslot}", mst)
                    if last_c and on_last_chunk_dt is not None:
                        on_last_chunk_dt(dt, mev)
            if after_loop_pe is not None:
                after_loop_pe()

        feedA, stateA = make_stats(Pb[0], "P0", Pb[1], "P1")
        GD = kt // 4
        xc_names = ("xc0", "xc1", "xc2", "xc3")
        for grp in range(4):
            mload = P.emit(
                "gpsimd",
                (lambda grp=grp: lambda g: g.dma_start(
                    hT[:, grp * GD * r:(grp + 1) * GD * r].rearrange(
                        "p (e r) -> p e r", e=GD),
                    xT[grp * GD * 128:(grp + 1) * GD * 128, :].rearrange(
                        "(e p) r -> p e r", e=GD)))(),
                waits=fr.need("hT") if grp == 0 else (),
                inc=xc_names[grp], n=16)
            for dt in range(grp * GD, (grp + 1) * GD):
                fr.wrote(("hTw", dt))
                feedA(dt, hT[:, dt * r:(dt + 1) * r], ("hTw", dt), mload)
        prefetch_wgu(0, 0)
        prefetch_wgu(0, 1)
        bcA = rows_h(stateA["last"], Pb[0], "P0", Pb[1], "P1")
        hmA = h_inplace(bcA)

        feedB, stateB = make_stats(Pb[1], "P1", Pb[3], "P3")
        xc_mile = {}
        LAG = 3

        def a_on_last_dt(dt, mev):
            sslot = dt % 2
            xc_mile[dt] = P.emit(
                "gpsimd",
                (lambda dt=dt, sslot=sslot: lambda g: g.tensor_copy(
                    hT[:, dt * r:(dt + 1) * r], st[sslot][:]))(),
                waits=(mev,) + fr.need("hT") + fr.need(("hTw", dt)),
                inc="pl")
            fr.wrote(("hTw", dt))
            fr.reader(f"stg{sslot}", xc_mile[dt])

        def a_on_last_dt_pre(dt):
            if dt >= LAG:
                dd = dt - LAG
                feedB(dd, hT[:, dd * r:(dd + 1) * r], ("hTw", dd), xc_mile[dd])

        def a_tail():
            for dd in range(kt - LAG, kt):
                feedB(dd, hT[:, dd * r:(dd + 1) * r], ("hTw", dd), xc_mile[dd])

        main_loop(0, xT, None, x1T, "x1T", hmA,
                  on_last_chunk_dt=a_on_last_dt,
                  on_last_chunk_dt_pre=a_on_last_dt_pre, after_loop_pe=a_tail)

        bc = rows_mixer(stateB["last"], Pb[1], "P1", Pb[3], "P3")
        hmB = h_inplace(bc)
        main_loop(1, x1T, "x1T", outT, "outT", hmB, res_affine=True)

        fin0 = ("st0", P.cnt["st0"])
        fin1 = ("st1", P.cnt["st1"])
        P.emit("sync", None, waits=(fin0, fin1))

        with nc.Block() as block:
            def emit_engine(eng_obj, name):
                waited = {}
                for fn, waits, inc in P.ops[name]:
                    for s, v in waits:
                        if waited.get(s, 0) < v:
                            eng_obj.wait_ge(sems[s], v)
                            waited[s] = v
                    inst = fn(eng_obj) if fn is not None else None
                    if inc is not None:
                        s, n0 = inc
                        assert inst is not None
                        inst.then_inc(sems[s], n0)

            @block.tensor
            def _(t):
                emit_engine(t, "tensor")

            @block.scalar
            def _(s):
                emit_engine(s, "scalar")

            @block.vector
            def _(v):
                emit_engine(v, "vector")

            @block.gpsimd
            def _(g):
                emit_engine(g, "gpsimd")

            @block.sync
            def _(s):
                emit_engine(s, "sync")

    return nc


def _prep_weights(Wg, Wu, Wdn, ft=FT, kt=KT):
    g = Wg.reshape(ft, 128, kt, 128).transpose(0, 3, 2, 1)
    u = Wu.reshape(ft, 128, kt, 128).transpose(0, 3, 2, 1)
    wgu = np.ascontiguousarray(np.stack([g, u], axis=2))
    wdt = np.ascontiguousarray(
        Wdn.reshape(kt, 128, ft, 128).transpose(0, 3, 2, 1))
    return wgu, wdt


def kernel(L0_attn_x, L0_ffn_W_gate, L0_ffn_W_up, L0_ffn_W_down):
    x = np.asarray(L0_attn_x, dtype=np.float32).reshape(8192, 4096)
    Wg = np.asarray(L0_ffn_W_gate, dtype=np.float32)
    Wu = np.asarray(L0_ffn_W_up, dtype=np.float32)
    Wdn = np.asarray(L0_ffn_W_down, dtype=np.float32)

    xT = np.ascontiguousarray(x.T)
    shards = [np.ascontiguousarray(xT[:, i * R:(i + 1) * R]) for i in range(NCORES)]
    wgu, wdt = _prep_weights(Wg, Wu, Wdn)

    nc = build_nc()
    in_maps = [{"xT": s, "wgu": wgu, "wd": wdt} for s in shards]
    res = run_bass_kernel_spmd(nc, in_maps, core_ids=list(range(NCORES)))
    outT = np.concatenate([np.asarray(r["outT"]) for r in res.results], axis=1)
    return np.ascontiguousarray(outT.T).reshape(1, 8192, 4096).astype(np.float32)


# version 17
# speedup vs baseline: 1.0698x; 1.0698x over previous
import numpy as np
from contextlib import ExitStack

import concourse.bass as bass
import concourse.mybir as mybir
from concourse.bass_utils import run_bass_kernel_spmd

F32 = mybir.dt.float32
BF16 = mybir.dt.bfloat16
AF = mybir.ActivationFunctionType
ALU = mybir.AluOpType

D = 4096
KT = D // 128
FD = 11008
FT = FD // 128
R = 1024
NCORES = 8
CHUNKS = (15, 15, 14, 14, 14, 14)
EPS = 1e-5

ENGINES = ("tensor", "scalar", "vector", "gpsimd", "sync")

DMA_SEMS = ("wgu0", "wgu1", "wd0", "wd1", "wd2", "xc0", "xc1", "xc2", "xc3",
            "xin0", "xin1", "yin0", "yin1", "yin2", "st0", "st1")
CMP_SEMS = ("pe", "act", "dve", "pl")


class Prog:

    def __init__(self):
        self.ops = {e: [] for e in ENGINES}
        self.cnt = {s: 0 for s in DMA_SEMS + CMP_SEMS}

    def emit(self, eng, fn=None, waits=(), inc=None, n=1):
        mile = None
        if inc is not None:
            if inc in DMA_SEMS and self.cnt[inc] > 0:
                waits = tuple(waits) + ((inc, self.cnt[inc]),)
            self.cnt[inc] += n
            mile = (inc, self.cnt[inc])
        w = {}
        for s, v in waits:
            if v is not None and (s not in w or v > w[s]):
                w[s] = v
        self.ops[eng].append((fn, tuple(w.items()), (inc, n) if inc else None))
        return mile


class Free:

    def __init__(self):
        self.d = {}

    def need(self, key):
        return tuple(self.d.get(key, {}).items())

    def reader(self, key, mile):
        if mile is None:
            return
        s, v = mile
        m = self.d.setdefault(key, {})
        m[s] = max(m.get(s, 0), v)

    def wrote(self, key):
        self.d[key] = {}


def build_nc(r=R, ft=FT, kt=KT, chunks=CHUNKS):
    assert sum(chunks) == ft
    rh_n = r // 512
    maxL = max(chunks)
    d = kt * 128

    nc = bass.Bass("TRN2", target_bir_lowering=False)

    xT = nc.dram_tensor("xT", [d, r], F32, kind="ExternalInput").ap()
    wgu = nc.dram_tensor("wgu", [ft, 128, 2, kt, 128], F32, kind="ExternalInput").ap()
    wd = nc.dram_tensor("wd", [kt, 128, ft, 128], F32, kind="ExternalInput").ap()
    outT = nc.dram_tensor("outT", [d, r], F32, kind="ExternalOutput").ap()
    x1T = nc.dram_tensor("x1T", [d, r], F32, kind="Internal").ap()
    yP = nc.dram_tensor("yP", [d, r], F32, kind="Internal").ap()

    P = Prog()
    fr = Free()
    store_mile = {}

    with ExitStack() as ctx:
        sb = lambda *a, **k: ctx.enter_context(nc.sbuf_tensor(*a, **k))
        ps = lambda *a: ctx.enter_context(nc.psum_tensor(*a))
        hT = sb("hT", [128, kt * r], BF16)
        gu = sb("gu", [128, maxL * r], BF16)
        wgus = [sb(f"wgus{i}", [128, 2 * kt * 128], BF16) for i in range(2)]
        wds = [sb(f"wds{i}", [128, maxL * 128], BF16) for i in range(3)]
        xin = [sb(f"xin{i}", [128, r], F32) for i in range(2)]
        sq = [sb(f"sq{i}", [128, r], BF16) for i in range(2)]
        silu = sb("silu", [128, r], F32)
        yin = [sb(f"yin{i}", [128, r], F32) for i in range(3)]
        st = [sb(f"stg{i}", [128, r], F32) for i in range(2)]
        mb = sb("mb", [128, r], F32)
        rsb = sb("rsb", [128, r], BF16)
        mb2 = sb("mb2", [128, r], F32)
        rsb2 = sb("rsb2", [128, r], F32)
        srows = sb("srows", [128, 4 * r], F32)
        onesb = sb("onesb", [128, 1], BF16)
        onesf = sb("onesf", [1, 128], F32)
        Pb = [ps(f"P{i}", [128, r], F32) for i in range(4)]

        sems = {}
        for s in DMA_SEMS + CMP_SEMS:
            sems[s] = ctx.enter_context(nc.semaphore(s))

        R0 = srows[0:1, 0:r]
        R1 = srows[0:1, r:2 * r]
        R2 = srows[0:1, 2 * r:3 * r]
        R3 = srows[0:1, 3 * r:4 * r]

        P.emit("vector", lambda v: v.memset(onesb[:], 1.0))
        m_init = P.emit("vector", lambda v: v.memset(onesf[:], 1.0), inc="dve")

        def make_stats(sumP, sumK, sqP, sqK):
            state = {"last": None}

            def feed(dt, a, akey, amile):
                slot = dt % 2
                msq = P.emit("scalar",
                             (lambda a=a, slot=slot: lambda s:
                              s.activation(sq[slot][:], a, AF.Square))(),
                             waits=(amile,) + fr.need(f"sq{slot}"), inc="act")
                fr.wrote(f"sq{slot}")
                fr.reader(akey, msq)
                mm = None
                for h in range(rh_n):
                    c0, c1 = h * 512, (h + 1) * 512
                    mm = P.emit("tensor",
                                (lambda a=a, c0=c0, c1=c1, dt=dt: lambda t:
                                 t.matmul(sumP[0:1, c0:c1], onesb[:], a[:, c0:c1],
                                          start=(dt == 0), stop=(dt == kt - 1),
                                          skip_group_check=True))(),
                                waits=(amile, m_init) + (fr.need(sumK) if dt == 0 and h == 0 else ()),
                                inc="pe" if h == rh_n - 1 else None)
                fr.reader(akey, mm)
                mm2 = None
                for h in range(rh_n):
                    c0, c1 = h * 512, (h + 1) * 512
                    mm2 = P.emit("tensor",
                                 (lambda slot=slot, c0=c0, c1=c1, dt=dt: lambda t:
                                  t.matmul(sqP[0:1, c0:c1], onesb[:], sq[slot][:, c0:c1],
                                           start=(dt == 0), stop=(dt == kt - 1),
                                           skip_group_check=True))(),
                                 waits=(msq,) + (fr.need(sqK) if dt == 0 and h == 0 else ()),
                                 inc="pe" if h == rh_n - 1 else None)
                fr.reader(f"sq{slot}", mm2)
                state["last"] = mm2
                if dt == 0:
                    fr.wrote(sumK)
                    fr.wrote(sqK)

            return feed, state

        def _row_ts(dst, src, s1, s2, op1, op2, waits, dk, sks=()):
            if op2 is None:
                fn = (lambda dst=dst, src=src: lambda v: v.tensor_scalar(
                    dst, src, s1, None, op1))()
            else:
                fn = (lambda dst=dst, src=src: lambda v: v.tensor_scalar(
                    dst, src, s1, s2, op1, op2))()
            m = P.emit("vector", fn, waits=waits + fr.need(dk), inc="dve")
            fr.wrote(dk)
            for sk in sks:
                fr.reader(sk, m)
            return m

        def _row_tt(dst, a, b, op, waits, dk, sks=()):
            m = P.emit("vector",
                       (lambda dst=dst, a=a, b=b: lambda v: v.tensor_tensor(
                           dst, a, b, op))(),
                       waits=waits + fr.need(dk), inc="dve")
            fr.wrote(dk)
            for sk in sks:
                fr.reader(sk, m)
            return m

        def _bcast(row, target, tkey, wait, via):
            viaP, viaK = via
            bm = None
            for h in range(rh_n):
                c0, c1 = h * 512, (h + 1) * 512
                bm = P.emit("tensor",
                            (lambda c0=c0, c1=c1, row=row, viaP=viaP: lambda t:
                             t.matmul(viaP[:, c0:c1], onesf[:], row[:, c0:c1],
                                      start=True, stop=True,
                                      skip_group_check=True))(),
                            waits=(wait, m_init) + (fr.need(viaK) if h == 0 else ()),
                            inc="pe" if h == rh_n - 1 else None)
            fr.wrote(viaK)
            cm = P.emit("vector",
                        (lambda target=target, viaP=viaP: lambda v:
                         v.tensor_copy(target[:], viaP[:]))(),
                        waits=(bm,) + fr.need(tkey), inc="dve")
            fr.wrote(tkey)
            fr.reader(viaK, cm)
            return cm

        def rows_h(last_mm, sumP, sumK, sqP, sqK):
            inv_n = 1.0 / float(d)
            mM = P.emit("vector", lambda v: v.tensor_scalar(
                R0, sumP[0:1, :], inv_n, None, ALU.mult),
                waits=(last_mm,) + fr.need("R0"), inc="dve")
            fr.wrote("R0")
            fr.reader(sumK, mM)
            mQ = P.emit("vector", lambda v: v.tensor_scalar(
                R1, sqP[0:1, :], inv_n, None, ALU.mult),
                waits=(last_mm,) + fr.need("R1"), inc="dve")
            fr.wrote("R1")
            fr.reader(sqK, mQ)
            m2 = _row_tt(R2, R0, R0, ALU.mult, (mM,), "R2")
            mV = _row_tt(R1, R1, R2, ALU.subtract, (mQ, m2), "R1", ("R2",))
            mVe = _row_ts(R2, R1, 1.0, EPS, ALU.mult, ALU.add, (mV,), "R2", ("R1",))
            mS = P.emit("scalar", lambda s: s.activation(R3, R2, AF.Sqrt),
                        waits=(mVe,) + fr.need("R3"), inc="act")
            fr.wrote("R3")
            fr.reader("R2", mS)
            mR = P.emit("vector", lambda v: v.reciprocal(R2, R3),
                        waits=(mS,) + fr.need("R2"), inc="dve")
            fr.wrote("R2")
            fr.reader("R3", mR)
            mE = _row_tt(R3, R0, R2, ALU.mult, (mR,), "R3", ("R0", "R2"))
            ca = _bcast(R2, rsb, "rsb", mR, (Pb[2], "P2"))
            fr.reader("R2", ca)
            cb = _bcast(R3, mb, "mb", mE, (Pb[3], "P3"))
            fr.reader("R3", cb)
            return ca, cb

        def rows_mixer(last_mm, sumP, sumK, sqP, sqK):
            inv_n = 1.0 / float(d)
            mM = P.emit("vector", lambda v: v.tensor_scalar(
                R0, sumP[0:1, :], inv_n, None, ALU.mult),
                waits=(last_mm,) + fr.need("R0"), inc="dve")
            fr.wrote("R0")
            fr.reader(sumK, mM)
            mQ = P.emit("vector", lambda v: v.tensor_scalar(
                R1, sqP[0:1, :], inv_n, None, ALU.mult),
                waits=(last_mm,) + fr.need("R1"), inc="dve")
            fr.wrote("R1")
            fr.reader(sqK, mQ)
            m2_ = _row_tt(R2, R0, R0, ALU.mult, (mM,), "R2")
            mV = _row_tt(R1, R1, R2, ALU.subtract, (mQ, m2_), "R1", ("R2",))
            mVe = _row_ts(R2, R1, 1.0, EPS, ALU.mult, ALU.add, (mV,), "R2", ("R1",))
            mS = P.emit("scalar", lambda s: s.activation(R3, R2, AF.Sqrt),
                        waits=(mVe,) + fr.need("R3"), inc="act")
            fr.wrote("R3")
            fr.reader("R2", mS)
            mR = P.emit("vector", lambda v: v.reciprocal(R2, R3),
                        waits=(mS,) + fr.need("R2"), inc="dve")
            fr.wrote("R2")
            fr.reader("R3", mR)
            m2rs = _row_ts(R3, R2, 2.0, None, ALU.mult, None, (mR,), "R3", ("R2",))
            mB = _row_tt(R2, R0, R3, ALU.mult, (m2rs,), "R2")
            mA = _row_ts(R3, R3, 1.0, None, ALU.add, None, (mB,), "R3")
            cA = _bcast(R3, rsb2, "rsb2", mA, (Pb[2], "P2"))
            cB = _bcast(R2, mb2, "mb2", mB, (Pb[3], "P3"))
            mm1 = _row_tt(R0, R0, R3, ALU.mult, (mA,), "R0")
            mm2 = _row_tt(R0, R0, R2, ALU.subtract, (mm1,), "R0")
            mv1 = _row_tt(R1, R1, R3, ALU.mult, (mV, mA), "R1")
            mv2 = _row_tt(R1, R1, R3, ALU.mult, (mv1,), "R1")
            mbm = _row_tt(R2, R2, R0, ALU.add, (mm2, cB), "R2", ("R0",))
            mve2 = _row_ts(R0, R1, 1.0, EPS, ALU.mult, ALU.add, (mv2,), "R0", ("R1",))
            mS2 = P.emit("scalar", lambda s: s.activation(R1, R0, AF.Sqrt),
                         waits=(mve2,) + fr.need("R1"), inc="act")
            fr.wrote("R1")
            fr.reader("R0", mS2)
            mR2 = P.emit("vector", lambda v: v.reciprocal(R0, R1),
                         waits=(mS2,) + fr.need("R0"), inc="dve")
            fr.wrote("R0")
            fr.reader("R1", mR2)
            mC = _row_tt(R3, R3, R0, ALU.mult, (mR2, cA), "R3", ("R0",))
            mE = _row_tt(R2, R2, R0, ALU.mult, (mbm, mR2), "R2", ("R0",))
            ca = _bcast(R3, rsb, "rsb", mC, (Pb[2], "P2"))
            fr.reader("R3", ca)
            cb = _bcast(R2, mb, "mb", mE, (Pb[3], "P3"))
            fr.reader("R2", cb)
            return ca, cb

        def h_inplace(bc_miles):
            ca, cb = bc_miles
            miles = []
            for dt in range(kt):
                hsl = hT[:, dt * r:(dt + 1) * r]
                m1 = P.emit("vector",
                            (lambda hsl=hsl: lambda v: v.tensor_tensor(
                                sq[0][:], hsl, rsb[:], ALU.mult))(),
                            waits=(ca,) + fr.need("sq0") + fr.need(("hTw", dt)),
                            inc="dve")
                fr.wrote("sq0")
                fr.reader("rsb", m1)
                m2 = P.emit("vector",
                            (lambda hsl=hsl: lambda v: v.tensor_tensor(
                                hsl, sq[0][:], mb[:], ALU.subtract))(),
                            waits=(m1, cb), inc="dve")
                fr.reader("sq0", m2)
                fr.reader("mb", m2)
                miles.append(m2)
            return miles

        pre_wgu = {}

        def emit_wgu(phase, ftile):
            if (phase, ftile) in pre_wgu:
                return pre_wgu.pop((phase, ftile))
            slot = ftile % 2
            m = P.emit(
                "gpsimd",
                (lambda ftile=ftile, slot=slot: lambda g:
                 g.dma_start(wgus[slot][:], wgu[ftile].rearrange(
                     "dd m k f -> dd (m k f)")))(),
                waits=fr.need(f"wgus{slot}"), inc=f"wgu{slot}", n=16)
            fr.wrote(f"wgus{slot}")
            return m

        def prefetch_wgu(phase, ftile):
            pre_wgu[(phase, ftile)] = emit_wgu(phase, ftile)

        def main_loop(phase, res_src, res_name, dst, dst_name, hT_miles,
                      res_affine=False, on_last_chunk_dt=None,
                      on_last_chunk_dt_pre=None, after_loop_pe=None):
            hT_last = hT_miles[-1]
            wd_mile = {}

            def wds_dma(c, base, L, dt):
                slot = dt % 3
                wd_mile[dt] = P.emit(
                    "gpsimd",
                    (lambda dt=dt, base=base, L=L, slot=slot: lambda g:
                     g.dma_start(wds[slot][:, 0:L * 128],
                                 wd[dt, :, base:base + L, :].rearrange(
                                     "ff l dd -> ff (l dd)")))(),
                    waits=fr.need(f"wds{slot}"), inc=f"wd{slot}", n=16)
                fr.wrote(f"wds{slot}")

            for c, L in enumerate(chunks):
                base = sum(chunks[:c])
                first_c, last_c = (c == 0), (c == len(chunks) - 1)
                fused = {}
                for j in range(L):
                    ftile = base + j
                    slot = ftile % 2
                    pg = Pb[0] if ftile % 2 == 0 else Pb[2]
                    pu = Pb[1] if ftile % 2 == 0 else Pb[3]
                    pgk = "P0" if ftile % 2 == 0 else "P2"
                    puk = "P1" if ftile % 2 == 0 else "P3"
                    if first_c and j == 0:
                        mw = [emit_wgu(phase, base), emit_wgu(phase, base + 1)]
                        tanks = [(0, "g", Pb[0], "P0", 0, 0), (0, "u", Pb[1], "P1", kt, 0),
                                 (1, "g", Pb[2], "P2", 0, 1), (1, "u", Pb[3], "P3", kt, 1)]
                        for k in range(kt):
                            for jj, which, pp, ppk, koff, wslot in tanks:
                                for h in range(rh_n):
                                    c0, c1 = h * 512, (h + 1) * 512
                                    if k == 0 and h == 0:
                                        w = (mw[jj], hT_miles[0]) + fr.need(ppk)
                                    elif h == 0 and jj == 0 and which == "g":
                                        w = (hT_miles[k],)
                                    else:
                                        w = ()
                                    fused[(jj, which)] = P.emit(
                                        "tensor",
                                        (lambda wslot=wslot, k=k, c0=c0, c1=c1,
                                                pp=pp, koff=koff: lambda t: t.matmul(
                                            pp[:, c0:c1],
                                            wgus[wslot][:, (koff + k) * 128:(koff + k + 1) * 128],
                                            hT[:, k * r + c0:k * r + c1],
                                            start=(k == 0), stop=(k == kt - 1),
                                            skip_group_check=True))(),
                                        waits=w,
                                        inc="pe" if (k == kt - 1 and h == rh_n - 1) else None)
                        for _, _, _, ppk, _, _ in tanks:
                            fr.wrote(ppk)
                    if first_c and j in (0, 1):
                        mg = fused[(j, "g")]
                        mu = fused[(j, "u")]
                    else:
                        mwgu = emit_wgu(phase, ftile)
                        for which, pp, ppk, koff in (("g", pg, pgk, 0),
                                                     ("u", pu, puk, kt)):
                            mlast = None
                            for k in range(kt):
                                for h in range(rh_n):
                                    c0, c1 = h * 512, (h + 1) * 512
                                    if k == 0 and h == 0:
                                        w = (mwgu, hT_last) + fr.need(ppk)
                                    else:
                                        w = ()
                                    mlast = P.emit(
                                        "tensor",
                                        (lambda slot=slot, k=k, c0=c0, c1=c1,
                                                pp=pp, koff=koff: lambda t: t.matmul(
                                            pp[:, c0:c1],
                                            wgus[slot][:, (koff + k) * 128:(koff + k + 1) * 128],
                                            hT[:, k * r + c0:k * r + c1],
                                            start=(k == 0), stop=(k == kt - 1),
                                            skip_group_check=True))(),
                                        waits=w,
                                        inc="pe" if (k == kt - 1 and h == rh_n - 1) else None)
                            fr.wrote(ppk)
                            if which == "g":
                                mg = mlast
                            else:
                                mu = mlast
                    fr.reader(f"wgus{slot}", mu)
                    fr.reader("hT", mu)
                    ms = P.emit("scalar",
                                (lambda pg=pg: lambda s: s.activation(
                                    silu[:], pg[:], AF.Silu))(),
                                waits=(mg,) + fr.need("silu"), inc="act")
                    fr.wrote("silu")
                    fr.reader(pgk, ms)
                    mgu = P.emit("vector",
                                 (lambda j=j, pu=pu: lambda v: v.tensor_tensor(
                                     gu[:, j * r:(j + 1) * r], silu[:], pu[:],
                                     ALU.mult))(),
                                 waits=(ms, mu) + (fr.need("gu") if j == 0 else ()),
                                 inc="dve")
                    if j == 0:
                        fr.wrote("gu")
                        gu_miles = []
                    gu_miles.append(mgu)
                    fr.reader("silu", mgu)
                    fr.reader(puk, mgu)
                    if j == L - 1:
                        gu_done = mgu
                        wds_dma(c, base, L, 0)
                        wds_dma(c, base, L, 1)
                        wds_dma(c, base, L, 2)
                        if not last_c:
                            prefetch_wgu(phase, base + L)
                            prefetch_wgu(phase, base + L + 1)
                        elif phase == 0:
                            prefetch_wgu(1, 0)
                            prefetch_wgu(1, 1)
                yin_mile = {}

                def yin_dma(dt):
                    yslot = dt % 3
                    yin_mile[dt] = P.emit(
                        "sync",
                        (lambda dt=dt, yslot=yslot: lambda s: s.dma_start(
                            yin[yslot][:], yP[dt * 128:(dt + 1) * 128, :]))(),
                        waits=fr.need(f"yin{yslot}") + (store_mile[("yP", dt)],),
                        inc=f"yin{yslot}", n=16)
                    fr.wrote(f"yin{yslot}")
                    fr.reader(("yP", dt), yin_mile[dt])

                for dt in range(kt):
                    slot = dt % 2
                    yslot = dt % 3
                    mwd = wd_mile[dt]
                    if not first_c:
                        if dt == 0:
                            yin_dma(0)
                            yin_dma(1)
                        if dt + 2 < kt:
                            yin_dma(dt + 2)
                        myin = yin_mile[dt]
                    if last_c:
                        dep = store_mile.get((res_name, dt))
                        mxin = P.emit(
                            "scalar",
                            (lambda dt=dt, slot=slot: lambda s: s.dma_start(
                                xin[slot][:],
                                res_src[dt * 128:(dt + 1) * 128, :]))(),
                            waits=fr.need(f"xin{slot}") + ((dep,) if dep else ()),
                            inc=f"xin{slot}", n=16)
                        fr.wrote(f"xin{slot}")
                        if res_affine:
                            q1 = P.emit("vector",
                                        (lambda slot=slot: lambda v:
                                         v.tensor_tensor(xin[slot][:], xin[slot][:],
                                                         rsb2[:], ALU.mult))(),
                                        waits=(mxin,), inc="dve")
                            fr.reader("rsb2", q1)
                            mxin = P.emit("vector",
                                          (lambda slot=slot: lambda v:
                                           v.tensor_tensor(xin[slot][:], xin[slot][:],
                                                           mb2[:], ALU.subtract))(),
                                          waits=(q1,), inc="dve")
                            fr.reader("mb2", mxin)
                    py = Pb[0] if dt % 2 == 0 else Pb[2]
                    pyk = "P0" if dt % 2 == 0 else "P2"
                    wslot = dt % 3
                    my = None
                    wz = (mwd, gu_miles[0] if dt == 0 else gu_done) + fr.need(pyk)
                    for j in range(L):
                        for h in range(rh_n):
                            c0, c1 = h * 512, (h + 1) * 512
                            if j == 0 and h == 0:
                                w = wz
                            elif dt == 0 and h == 0:
                                w = (gu_miles[j],)
                            else:
                                w = ()
                            my = P.emit(
                                "tensor",
                                (lambda wslot=wslot, j=j, c0=c0, c1=c1, py=py:
                                 lambda t: t.matmul(
                                     py[:, c0:c1],
                                     wds[wslot][:, j * 128:(j + 1) * 128],
                                     gu[:, j * r + c0:j * r + c1],
                                     start=(j == 0), stop=(j == L - 1),
                                     skip_group_check=True))(),
                                waits=w,
                                inc="pe" if (j == L - 1 and h == rh_n - 1) else None)
                    fr.wrote(pyk)
                    fr.reader(f"wds{wslot}", my)
                    fr.reader("gu", my)
                    if dt + 3 < kt:
                        wds_dma(c, base, L, dt + 3)
                    if last_c and on_last_chunk_dt_pre is not None:
                        on_last_chunk_dt_pre(dt)
                    sslot = dt % 2
                    if first_c and not last_c:
                        mev = P.emit("scalar",
                                     (lambda py=py, sslot=sslot: lambda s:
                                      s.activation(st[sslot][:], py[:], AF.Copy,
                                                   scale=0.5))(),
                                     waits=(my,) + fr.need(f"stg{sslot}"),
                                     inc="act")
                        fr.wrote(f"stg{sslot}")
                        fr.reader(pyk, mev)
                    else:
                        hbuf, hkey = (silu, "silu") if dt % 2 == 0 else (mb, "mb")
                        mhalf = P.emit("scalar",
                                       (lambda py=py, hbuf=hbuf: lambda s:
                                        s.activation(hbuf[:], py[:], AF.Copy,
                                                     scale=0.5))(),
                                       waits=(my,) + fr.need(hkey), inc="act")
                        fr.wrote(hkey)
                        fr.reader(pyk, mhalf)
                        if not last_c:
                            mev = P.emit("vector",
                                         (lambda sslot=sslot, yslot=yslot, hbuf=hbuf:
                                          lambda v: v.tensor_tensor(
                                              st[sslot][:], hbuf[:], yin[yslot][:],
                                              ALU.add))(),
                                         waits=(mhalf, myin) + fr.need(f"stg{sslot}"),
                                         inc="dve")
                            fr.wrote(f"stg{sslot}")
                            fr.reader(f"yin{yslot}", mev)
                            fr.reader(hkey, mev)
                        else:
                            if not first_c:
                                ma = P.emit("vector",
                                            (lambda yslot=yslot, hbuf=hbuf: lambda v:
                                             v.tensor_tensor(hbuf[:], hbuf[:],
                                                             yin[yslot][:], ALU.add))(),
                                            waits=(mhalf, myin), inc="dve")
                                fr.reader(f"yin{yslot}", ma)
                            else:
                                ma = mhalf
                            mev = P.emit("vector",
                                         (lambda sslot=sslot, slot=slot, hbuf=hbuf:
                                          lambda v: v.tensor_tensor(
                                              st[sslot][:], hbuf[:],
                                              xin[slot][:], ALU.add))(),
                                         waits=(ma, mxin) + fr.need(f"stg{sslot}"),
                                         inc="dve")
                            fr.reader(f"xin{slot}", mev)
                            fr.reader(hkey, mev)
                            fr.wrote(f"stg{sslot}")
                    tgt, tname = (yP, "yP") if not last_c else (dst, dst_name)
                    wst = fr.need((tname, dt)) if tname == "yP" else ()
                    mst = P.emit(
                        "sync",
                        (lambda tgt=tgt, dt=dt, sslot=sslot: lambda s:
                         s.dma_start(tgt[dt * 128:(dt + 1) * 128, :],
                                     st[sslot][:]))(),
                        waits=(mev,) + wst, inc=f"st{sslot}", n=16)
                    store_mile[(tname, dt)] = mst
                    if tname == "yP":
                        fr.wrote((tname, dt))
                    fr.reader(f"stg{sslot}", mst)
                    if last_c and on_last_chunk_dt is not None:
                        on_last_chunk_dt(dt, mev)
            if after_loop_pe is not None:
                after_loop_pe()

        feedA, stateA = make_stats(Pb[0], "P0", Pb[1], "P1")
        GD = kt // 4
        xc_names = ("xc0", "xc1", "xc2", "xc3")
        for grp in range(4):
            mload = P.emit(
                "gpsimd",
                (lambda grp=grp: lambda g: g.dma_start(
                    hT[:, grp * GD * r:(grp + 1) * GD * r].rearrange(
                        "p (e r) -> p e r", e=GD),
                    xT[grp * GD * 128:(grp + 1) * GD * 128, :].rearrange(
                        "(e p) r -> p e r", e=GD)))(),
                waits=fr.need("hT") if grp == 0 else (),
                inc=xc_names[grp], n=16)
            for dt in range(grp * GD, (grp + 1) * GD):
                fr.wrote(("hTw", dt))
                feedA(dt, hT[:, dt * r:(dt + 1) * r], ("hTw", dt), mload)
        prefetch_wgu(0, 0)
        prefetch_wgu(0, 1)
        bcA = rows_h(stateA["last"], Pb[0], "P0", Pb[1], "P1")
        hmA = h_inplace(bcA)

        feedB, stateB = make_stats(Pb[1], "P1", Pb[3], "P3")
        xc_mile = {}
        LAG = 3

        def a_on_last_dt(dt, mev):
            sslot = dt % 2
            xc_mile[dt] = P.emit(
                "gpsimd",
                (lambda dt=dt, sslot=sslot: lambda g: g.tensor_copy(
                    hT[:, dt * r:(dt + 1) * r], st[sslot][:]))(),
                waits=(mev,) + fr.need("hT") + fr.need(("hTw", dt)),
                inc="pl")
            fr.wrote(("hTw", dt))
            fr.reader(f"stg{sslot}", xc_mile[dt])

        def a_on_last_dt_pre(dt):
            if dt >= LAG:
                dd = dt - LAG
                feedB(dd, hT[:, dd * r:(dd + 1) * r], ("hTw", dd), xc_mile[dd])

        def a_tail():
            for dd in range(kt - LAG, kt):
                feedB(dd, hT[:, dd * r:(dd + 1) * r], ("hTw", dd), xc_mile[dd])

        main_loop(0, xT, None, x1T, "x1T", hmA,
                  on_last_chunk_dt=a_on_last_dt,
                  on_last_chunk_dt_pre=a_on_last_dt_pre, after_loop_pe=a_tail)

        bc = rows_mixer(stateB["last"], Pb[1], "P1", Pb[3], "P3")
        hmB = h_inplace(bc)
        main_loop(1, x1T, "x1T", outT, "outT", hmB, res_affine=True)

        fin0 = ("st0", P.cnt["st0"])
        fin1 = ("st1", P.cnt["st1"])
        P.emit("sync", None, waits=(fin0, fin1))

        with nc.Block() as block:
            def emit_engine(eng_obj, name):
                waited = {}
                for fn, waits, inc in P.ops[name]:
                    for s, v in waits:
                        if waited.get(s, 0) < v:
                            eng_obj.wait_ge(sems[s], v)
                            waited[s] = v
                    inst = fn(eng_obj) if fn is not None else None
                    if inc is not None:
                        s, n0 = inc
                        assert inst is not None
                        inst.then_inc(sems[s], n0)

            @block.tensor
            def _(t):
                emit_engine(t, "tensor")

            @block.scalar
            def _(s):
                emit_engine(s, "scalar")

            @block.vector
            def _(v):
                emit_engine(v, "vector")

            @block.gpsimd
            def _(g):
                emit_engine(g, "gpsimd")

            @block.sync
            def _(s):
                emit_engine(s, "sync")

    return nc


def _prep_weights(Wg, Wu, Wdn, ft=FT, kt=KT):
    g = Wg.reshape(ft, 128, kt, 128).transpose(0, 3, 2, 1)
    u = Wu.reshape(ft, 128, kt, 128).transpose(0, 3, 2, 1)
    wgu = np.ascontiguousarray(np.stack([g, u], axis=2))
    wdt = np.ascontiguousarray(
        Wdn.reshape(kt, 128, ft, 128).transpose(0, 3, 2, 1))
    return wgu, wdt


def kernel(L0_attn_x, L0_ffn_W_gate, L0_ffn_W_up, L0_ffn_W_down):
    x = np.asarray(L0_attn_x, dtype=np.float32).reshape(8192, 4096)
    Wg = np.asarray(L0_ffn_W_gate, dtype=np.float32)
    Wu = np.asarray(L0_ffn_W_up, dtype=np.float32)
    Wdn = np.asarray(L0_ffn_W_down, dtype=np.float32)

    xT = np.ascontiguousarray(x.T)
    shards = [np.ascontiguousarray(xT[:, i * R:(i + 1) * R]) for i in range(NCORES)]
    wgu, wdt = _prep_weights(Wg, Wu, Wdn)

    nc = build_nc()
    in_maps = [{"xT": s, "wgu": wgu, "wd": wdt} for s in shards]
    res = run_bass_kernel_spmd(nc, in_maps, core_ids=list(range(NCORES)))
    outT = np.concatenate([np.asarray(r["outT"]) for r in res.results], axis=1)
    return np.ascontiguousarray(outT.T).reshape(1, 8192, 4096).astype(np.float32)
